# version 01
# baseline (speedup 1.0000x reference)
import numpy as np
import ml_dtypes
import concourse.bass as bass
import concourse.mybir as mybir
from concourse.bass_utils import run_bass_kernel_spmd

F32 = mybir.dt.float32
BF16 = mybir.dt.bfloat16
I32 = mybir.dt.int32
AF = mybir.ActivationFunctionType
ALU = mybir.AluOpType
bfdt = ml_dtypes.bfloat16

D = 4096
I = 11008
KC = D // 128
IC = I // 128
T = 512
NSP = 2
TOK = T * NSP
EPS = 1e-5

ENGS = ("sync", "tensor", "vector", "scalar", "gpsimd")


class Prog:

    def __init__(self):
        self.streams = {e: [] for e in ENGS}
        self.cnt = {}
        self.marks = {}

    def mark(self, name):
        self.marks[name] = ({e: len(st) for e, st in self.streams.items()},
                            dict(self.cnt))

    def wait(self, eng, sem, val):
        assert val <= self.cnt.get(sem, 0) + 10**9
        self.streams[eng].append(("w", sem, int(val)))

    def op(self, eng, fn, sem=None, inc=0):
        if sem is not None and inc:
            self.cnt[sem] = self.cnt.get(sem, 0) + inc
        self.streams[eng].append(("o", fn, sem, inc))

    def val(self, sem):
        return self.cnt.get(sem, 0)

    def replay(self, eng_name, eng_obj, sems, regs, cut=None):
        lim = cut[eng_name] if cut is not None else len(self.streams[eng_name])
        for item in self.streams[eng_name][:lim]:
            if item[0] == "w":
                _, s, v = item
                eng_obj.wait_ge(sems[s], v)
            else:
                _, fn, s, inc = item
                ins = fn(eng_obj, regs)
                if s is not None and inc:
                    ins.then_inc(sems[s], inc)


def build_program(cut=None, dump=None):
    nc = bass.Bass("TRN2", target_bir_lowering=False, num_devices=8,
                   dynamic_dma_scratch_size=16384)

    xt_in = nc.dram_tensor("xt", [NSP, 128, KC, T], F32, kind="ExternalInput")
    w_sl = nc.dram_tensor("w_sl", [2048, D], F32, kind="ExternalInput")
    mask_sl = nc.dram_tensor("mask_sl", [2048, D], I32, kind="ExternalInput")
    wg_sl = nc.dram_tensor("wg_sl", [5504, D], F32, kind="ExternalInput")
    wu_sl = nc.dram_tensor("wu_sl", [5504, D], F32, kind="ExternalInput")
    wd_sl = nc.dram_tensor("wd_sl", [2048, I], F32, kind="ExternalInput")
    bias_in = nc.dram_tensor("bias_img", [128, KC], F32, kind="ExternalInput")
    selbig_in = nc.dram_tensor("selbig", [128, 16 * 128], BF16, kind="ExternalInput")
    selexp_in = nc.dram_tensor("selexp", [128, 16 * 128], BF16, kind="ExternalInput")
    ident_in = nc.dram_tensor("ident_bf", [128, 128], BF16, kind="ExternalInput")
    onesc_in = nc.dram_tensor("ones_col", [128, 1], BF16, kind="ExternalInput")
    onesr_in = nc.dram_tensor("ones_row", [1, 128], F32, kind="ExternalInput")
    out_img = nc.dram_tensor("out_img", [NSP, KC, 128, T], F32, kind="ExternalOutput")

    attn_img = nc.dram_tensor("attn_img", [32, 128, D], BF16, addr_space="Shared")
    wg_img = nc.dram_tensor("wg_img", [IC, 128, D], BF16, addr_space="Shared")
    wu_img = nc.dram_tensor("wu_img", [IC, 128, D], BF16, addr_space="Shared")
    wd_img = nc.dram_tensor("wd_img", [32, 128, I], BF16, addr_space="Shared")
    x2scr = nc.dram_tensor("x2scr", [128, KC * T], F32)
    attn_loc = nc.dram_tensor("attn_loc", [16, 128, D], BF16)
    wg_loc = nc.dram_tensor("wg_loc", [43, 128, D], BF16)
    wu_loc = nc.dram_tensor("wu_loc", [43, 128, D], BF16)
    wd_loc = nc.dram_tensor("wd_loc", [16, 128, I], BF16)

    from contextlib import ExitStack
    with ExitStack() as ctx:
        sb = lambda *a: ctx.enter_context(nc.sbuf_tensor(*a))
        psumt = lambda *a: ctx.enter_context(nc.psum_tensor(*a))
        mksem = lambda n: ctx.enter_context(nc.semaphore(n))

        AX = sb("AX", [128, 16384], F32)
        AH = sb("AH", [128, 16384], BF16)
        AG = sb("AG", [128, 53248], BF16)
        bias_sb = sb("bias_sb", [128, KC], F32)
        ident_sb = sb("ident_sb", [128, 128], BF16)
        onesc_sb = sb("onesc_sb", [128, 1], BF16)
        onesr_sb = sb("onesr_sb", [1, 128], F32)
        pb = [psumt(f"pb{i}", [128, 512], F32) for i in range(8)]

        sem_names = ["QW", "QG", "PE", "DV", "AC"]
        sems = {n: mksem(n) for n in sem_names}

        def axf(a, n):
            return AX.ap()[:, a:a + n]

        def axb(a, n):
            return AX.ap()[:, a:a + n].bitcast(BF16)

        def ahb(a, n):
            return AH.ap()[:, a:a + n]

        def agb(a, n):
            return AG.ap()[:, a:a + n]

        def agf(a, n):
            assert a % 2 == 0 and n % 2 == 0
            return AG.ap()[:, a:a + n].bitcast(F32)

        def pbb(i):
            return pb[i].ap().bitcast(BF16)

        p = Prog()

        def g_init(eng, regs):
            pid = eng.partition_id()
            regs["rank"] = eng.alloc_register("rank")
            eng.reg_alu(regs["rank"], pid, 2, ALU.mod)
            return None
        p.op("gpsimd", g_init)

        _LOC = {id(attn_img): attn_loc, id(wg_img): wg_loc,
                id(wu_img): wu_loc, id(wd_img): wd_loc}

        def img_store(handle, base_half, idx, src_ap, nelem):
            loc = _LOC[id(handle)]
            def fn(eng, regs):
                return eng.dma_start(out=loc.ap()[idx], in_=src_ap)
            p.op("gpsimd", fn, "QG", 16)

        def img_publish(handle, base_half, nimg, nelem):
            loc = _LOC[id(handle)]
            p.wait("gpsimd", "QG", p.val("QG"))
            def fn(eng, regs):
                return eng.dma_start(
                    out=bass.AP(handle,
                                bass.RuntimeValue(regs["rank"]) * base_half,
                                [[1, nimg * nelem]]),
                    in_=loc.ap().flatten())
            p.op("gpsimd", fn, "QG", 16)

        for (dst, src) in [
            (agb(0, 2048), selbig_in.ap()),
            (agb(2048, 2048), selexp_in.ap()),
            (bias_sb[:], bias_in.ap()),
            (ident_sb[:], ident_in.ap()),
            (onesc_sb[:], onesc_in.ap()),
            (onesr_sb[:], onesr_in.ap()),
        ]:
            p.op("sync", (lambda d, s: lambda e, r: e.dma_start(out=d, in_=s))(dst, src),
                 "QW", 16)
        qw_consts = p.val("QW")
        p.mark("prep_consts")
        selbig = agb(0, 2048)
        selexp = agb(2048, 2048)

        mslab = [AX.ap()[:, 0:4096].bitcast(I32), AX.ap()[:, 4096:8192].bitcast(I32)]
        mbf = [ahb(0, 4096), ahb(4096, 4096)]
        cast_done = []
        pe_colsum_r = []
        for r in range(16):
            if r >= 2:
                p.wait("sync", "DV", cast_done[r - 2])
            p.op("sync", (lambda rr: lambda e, g: e.dma_start(
                out=mslab[rr % 2], in_=mask_sl.ap()[rr * 128:(rr + 1) * 128, :]))(r),
                "QW", 16)
            p.wait("vector", "QW", p.val("QW"))
            if r >= 2:
                p.wait("vector", "PE", pe_colsum_r[r - 2])
            p.op("vector", (lambda rr: lambda e, g: e.tensor_copy(
                out=mbf[rr % 2], in_=mslab[rr % 2]))(r), "DV", 1)
            cast_done.append(p.val("DV"))
            p.wait("tensor", "DV", p.val("DV"))
            for s in range(8):
                p.op("tensor", (lambda rr, ss: lambda e, g: e.matmul(
                    pb[ss][:], selbig[:, rr * 128:(rr + 1) * 128],
                    mbf[rr % 2][:, ss * 512:(ss + 1) * 512],
                    start=(rr == 0), stop=(rr == 15)))(r, s),
                    "PE", 1 if s == 7 else 0)
            pe_colsum_r.append(p.val("PE"))
        pe_colsum = p.val("PE")
        colsum = axf(8192, 4096)
        p.wait("vector", "PE", pe_colsum)
        for s in range(8):
            p.op("vector", (lambda ss: lambda e, g: e.tensor_copy(
                out=colsum[:, ss * 512:(ss + 1) * 512], in_=pb[ss][:]))(s), "DV", 1)
        S_t = axf(12288, 128)
        p.op("vector", lambda e, g: e.reduce_sum(
            out=S_t, in_=colsum.rearrange("p (a b) -> p a b", b=32),
            axis=mybir.AxisListType.X), "DV", 1)
        bm01 = ahb(8192, 128)
        p.op("vector", lambda e, g: e.tensor_single_scalar(
            out=bm01, in_=S_t, scalar=0.0, op=ALU.is_gt), "DV", 1)
        dv_bm01 = p.val("DV")
        p.mark("prep_bm01")

        def emit_transposes(src_bf_ap, nchunk, img_asm_ap, tr_state):
            filled = 0
            c0 = 0
            for c in range(nchunk):
                b = tr_state["bank"]
                q = tr_state["slot"]
                if q == 0:
                    last = tr_state["bank_evac"][b]
                    if last is not None:
                        p.wait("tensor", "DV", last)
                last_chunk = (c == nchunk - 1)
                batch_end = (q == 7) or last_chunk
                p.op("tensor", (lambda bb, qq, cc: lambda e, g: e.transpose(
                    pbb(bb)[:, qq * 128:(qq + 1) * 128],
                    src_bf_ap[:, cc * 128:(cc + 1) * 128], ident_sb[:]))(b, q, c),
                    "PE", 1 if batch_end else 0)
                tr_state["slot"] += 1
                if batch_end:
                    nsl = tr_state["slot"]
                    pe_now = p.val("PE")
                    p.wait("vector", "PE", pe_now)
                    p.op("vector", (lambda bb, cc0, nn: lambda e, g: e.tensor_copy(
                        out=img_asm_ap[:, cc0 * 128:(cc0 + nn) * 128],
                        in_=pbb(bb)[:, 0:nn * 128]))(b, c0, nsl), "DV", 1)
                    tr_state["bank_evac"][b] = p.val("DV")
                    tr_state["bank"] = (b + 1) % 4
                    tr_state["slot"] = 0
                    c0 = c + 1
            return p.val("DV")

        tr_state = {"bank": 0, "slot": 0, "bank_evac": [None] * 4}

        wslab = [axf(0, 4096), axf(4096, 4096)]
        wmask = [ahb(0, 4096), ahb(4096, 4096)]
        imga = [agb(4096, 4096), agb(8192, 4096)]
        nmt = axf(12416, 128)
        slab_user = [dv_bm01, dv_bm01]
        img_user = [0, 0]
        wmask_user = [0, 0]
        dv_nmt_prev = None
        for gl in range(16):
            bi = gl % 2
            p.wait("sync", "DV", slab_user[bi])
            p.op("sync", (lambda g_: lambda e, r: e.dma_start(
                out=wslab[g_ % 2], in_=w_sl.ap()[g_ * 128:(g_ + 1) * 128, :]))(gl),
                "QW", 16)
            qw_here = p.val("QW")
            p.wait("tensor", "DV", dv_bm01)
            if dv_nmt_prev is not None:
                p.wait("tensor", "DV", dv_nmt_prev)
            p.op("tensor", (lambda g_: lambda e, r: e.matmul(
                pb[6][:, 0:128], selexp[:, g_ * 128:(g_ + 1) * 128], bm01,
                start=True, stop=True))(gl), "PE", 1)
            p.wait("vector", "PE", p.val("PE"))
            p.op("vector", lambda e, r: e.tensor_copy(out=nmt, in_=pb[6][:, 0:128]),
                 "DV", 1)
            p.wait("vector", "QW", qw_here)
            p.wait("vector", "PE", wmask_user[bi])
            p.op("vector", (lambda bi_: lambda e, r: e.tensor_tensor(
                out=wmask[bi_].rearrange("p (a b) -> p a b", b=32),
                in0=wslab[bi_].rearrange("p (a b) -> p a b", b=32),
                in1=nmt.unsqueeze(2).broadcast_to([128, 128, 32]),
                op=ALU.mult))(bi), "DV", 1)
            dv_nmt_prev = p.val("DV")
            slab_user[bi] = p.val("DV")
            if gl == 0:
                p.mark("prep_g0")
            p.wait("tensor", "DV", p.val("DV"))
            p.wait("tensor", "QG", img_user[bi])
            dvv = emit_transposes(wmask[bi], KC, imga[bi], tr_state)
            wmask_user[bi] = p.val("PE")
            p.wait("gpsimd", "DV", dvv)
            img_store(attn_img, 16 * 128 * D, gl, imga[bi], 128 * D)
            img_user[bi] = p.val("QG")

        img_publish(attn_img, 16 * 128 * D, 16, 128 * D)
        p.mark("prep_attn")
        for (w_in, img_t) in ((wg_sl, wg_img), (wu_sl, wu_img)):
            for gl in range(43):
                bi = gl % 2
                p.wait("sync", "DV", slab_user[bi])
                p.op("sync", (lambda g_, wi: lambda e, r: e.dma_start(
                    out=wslab[g_ % 2], in_=wi.ap()[g_ * 128:(g_ + 1) * 128, :]))(gl, w_in),
                    "QW", 16)
                p.wait("vector", "QW", p.val("QW"))
                p.wait("vector", "PE", wmask_user[bi])
                p.op("vector", (lambda bi_: lambda e, r: e.tensor_copy(
                    out=wmask[bi_], in_=wslab[bi_]))(bi), "DV", 1)
                slab_user[bi] = p.val("DV")
                p.wait("tensor", "DV", p.val("DV"))
                p.wait("tensor", "QG", img_user[bi])
                dvv = emit_transposes(wmask[bi], KC, imga[bi], tr_state)
                wmask_user[bi] = p.val("PE")
                p.wait("gpsimd", "DV", dvv)
                img_store(img_t, 43 * 128 * D, gl, imga[bi], 128 * D)
                img_user[bi] = p.val("QG")
                if img_t is wg_img and gl in (0, 1, 5, 20):
                    p.mark(f"prep_wg{gl}")
            img_publish(img_t, 43 * 128 * D, 43, 128 * D)

        p.mark("prep_wgwu")
        wdslab = [axf(0, 5504), axf(5504, 5504)]
        wdbf = [ahb(0, 5504), ahb(5504, 5504)]
        imgd = [agb(4096, 11008), agb(15104, 11008)]
        dslab_user = [slab_user[0], slab_user[1]]
        dimg_user = [img_user[0], img_user[1]]
        wdbf_user = [max(wmask_user), max(wmask_user)]
        for gl in range(16):
            bi = gl % 2
            p.wait("tensor", "QG", dimg_user[bi])
            for hf in range(2):
                hb = hf
                p.wait("sync", "DV", dslab_user[hb])
                p.op("sync", (lambda g_, h_: lambda e, r: e.dma_start(
                    out=wdslab[h_], in_=wd_sl.ap()[g_ * 128:(g_ + 1) * 128,
                                                   h_ * 5504:(h_ + 1) * 5504]))(gl, hf),
                    "QW", 16)
                p.wait("vector", "QW", p.val("QW"))
                p.wait("vector", "PE", wdbf_user[hf])
                p.op("vector", (lambda h_: lambda e, r: e.tensor_copy(
                    out=wdbf[h_], in_=wdslab[h_]))(hf), "DV", 1)
                dslab_user[hb] = p.val("DV")
                p.wait("tensor", "DV", p.val("DV"))
                emit_transposes(wdbf[hf], 43,
                                imgd[bi][:, hf * 5504:(hf + 1) * 5504], tr_state)
                wdbf_user[hf] = p.val("PE")
            dvv = p.val("DV")
            p.wait("gpsimd", "DV", dvv)
            img_store(wd_img, 16 * 128 * I, gl, imgd[bi], 128 * I)
            dimg_user[bi] = p.val("QG")

        img_publish(wd_img, 16 * 128 * I, 16, 128 * I)
        p.mark("prep_mask")
        p.wait("gpsimd", "QG", p.val("QG"))
        p.wait("sync", "QW", p.val("QW"))
        p.mark("prep_end")

        prep_streams = p.streams
        prep_counts = dict(p.cnt)

        p2 = Prog()
        p2.cnt = dict(prep_counts)

        xT = axf(0, 16384)
        h1T = ahb(0, 16384)
        aT = agf(0, 32768)
        wbuf = [agb(32768, 4096), agb(36864, 4096)]
        tmp = agf(44032, 1024)
        sbf = [agb(45056, 512), agb(45568, 512)]
        ssq = [agb(46080, 512), agb(46592, 512)]
        mb_sb = agf(47104, 1024)
        rb_sb = agf(48128, 1024)
        m_row = agf(49152, 1024)[0:1, :]
        msq_row = agf(50176, 1024)[0:1, :]
        var_row = agf(51200, 1024)[0:1, :]
        s_row = agf(50176, 1024)[0:1, :]
        r_row = agf(52224, 1024)[0:1, :]

        def chunk(ap, c, w=512):
            return ap[:, c * w:(c + 1) * w]

        def emit_ln_stats(src_ap, src_is_bf16, tag):
            for c in range(KC):
                bi = c % 2
                if src_is_bf16:
                    rhs_sum = chunk(src_ap, c)
                else:
                    if c >= 2:
                        p2.wait("vector", "PE", pe_done[c - 2])
                    p2.op("vector", (lambda cc, b_: lambda e, r: e.tensor_copy(
                        out=sbf[b_], in_=chunk(src_ap, cc)))(c, bi), "DV", 1)
                    rhs_sum = sbf[bi]
                if c >= 2:
                    p2.wait("scalar", "PE", pe_done[c - 2])
                p2.op("scalar", (lambda cc, b_: lambda e, r: e.activation(
                    out=ssq[b_], in_=chunk(src_ap, cc), func=AF.Square))(c, bi),
                    "AC", 1)
                if not src_is_bf16:
                    p2.wait("tensor", "DV", p2.val("DV"))
                p2.wait("tensor", "AC", p2.val("AC"))
                p2.op("tensor", (lambda cc, rs: lambda e, r: e.matmul(
                    pb[4][0:1, :], onesc_sb[:], rs,
                    start=(cc == 0), stop=(cc == KC - 1)))(c, rhs_sum), "PE", 0)
                p2.op("tensor", (lambda cc, b_: lambda e, r: e.matmul(
                    pb[5][0:1, :], onesc_sb[:], ssq[b_],
                    start=(cc == 0), stop=(cc == KC - 1)))(c, bi), "PE", 1)
                pe_done.append(p2.val("PE"))
            pe_stats = p2.val("PE")
            p2.wait("scalar", "PE", pe_stats)
            p2.op("scalar", lambda e, r: e.mul(m_row, pb[4][0:1, :], 1.0 / D), "AC", 1)
            p2.op("scalar", lambda e, r: e.activation(
                out=msq_row, in_=m_row, func=AF.Square), "AC", 1)
            p2.wait("vector", "PE", pe_stats)
            p2.op("vector", lambda e, r: e.tensor_scalar(
                out=var_row, in0=pb[5][0:1, :], scalar1=1.0 / D, scalar2=None,
                op0=ALU.mult), "DV", 1)
            p2.wait("vector", "AC", p2.val("AC"))
            p2.op("vector", lambda e, r: e.tensor_tensor(
                out=var_row, in0=var_row, in1=msq_row, op=ALU.subtract), "DV", 1)
            p2.op("vector", lambda e, r: e.tensor_single_scalar(
                out=var_row, in_=var_row, scalar=float(EPS), op=ALU.add), "DV", 1)
            p2.wait("scalar", "DV", p2.val("DV"))
            p2.op("scalar", lambda e, r: e.activation(
                out=s_row, in_=var_row, func=AF.Sqrt), "AC", 1)
            p2.wait("vector", "AC", p2.val("AC"))
            p2.op("vector", lambda e, r: e.reciprocal(r_row, s_row), "DV", 1)
            dv_rows = p2.val("DV")
            p2.wait("tensor", "DV", dv_rows)
            p2.op("tensor", lambda e, r: e.matmul(
                pb[6][:], onesr_sb[:], m_row, start=True, stop=True), "PE", 1)
            p2.op("tensor", lambda e, r: e.matmul(
                pb[7][:], onesr_sb[:], r_row, start=True, stop=True), "PE", 1)
            p2.wait("vector", "PE", p2.val("PE"))
            p2.op("vector", lambda e, r: e.tensor_copy(out=mb_sb, in_=pb[6][:]), "DV", 1)
            p2.op("vector", lambda e, r: e.tensor_copy(out=rb_sb, in_=pb[7][:]), "DV", 1)
            return p2.val("DV")

        qg_out_total = 0
        last_sp = {}
        for sp in range(NSP):
            if sp > 0:
                p2.wait("sync", "QG", last_sp["out_stores"])
                p2.wait("sync", "DV", last_sp["dv_final"])
                p2.wait("sync", "PE", last_sp["pe_down"])
            p2.op("sync", (lambda s_: lambda e, r: e.dma_start(
                out=xT.rearrange("p (c t) -> p c t", t=T), in_=xt_in.ap()[s_]))(sp),
                "QW", 16)
            qw_xt = p2.val("QW")
            pe_done = []
            if sp > 0:
                p2.wait("vector", "DV", last_sp["dv_final"])
            p2.wait("vector", "QW", qw_xt)
            p2.wait("scalar", "QW", qw_xt)
            emit_ln_stats(xT, False, "ln1")
            dv_norm_base = p2.val("DV")
            for c in range(KC):
                p2.op("vector", (lambda cc: lambda e, r: e.tensor_tensor(
                    out=tmp, in0=chunk(xT, cc), in1=mb_sb, op=ALU.subtract))(c), "DV", 1)
                p2.op("vector", (lambda cc: lambda e, r: e.tensor_tensor(
                    out=chunk(h1T, cc), in0=tmp, in1=rb_sb, op=ALU.mult))(c), "DV", 1)
            dv_h1 = p2.val("DV")
            p2.mark(f"sp{sp}_A")

            bank_free = [None] * 4
            wb_free = [None, None]
            qw_grp = []
            if sp > 0:
                p2.wait("sync", "PE", last_sp["pe_down"])
            for g in range(32):
                bi = g % 2
                if wb_free[bi] is not None:
                    p2.wait("sync", "PE", wb_free[bi])
                p2.op("sync", (lambda g_, b_: lambda e, r: e.dma_start(
                    out=wbuf[b_], in_=attn_img.ap()[g_]))(g, bi), "QW", 16)
                qw_grp.append(p2.val("QW"))
                b = g % 4
                p2.wait("tensor", "QW", qw_grp[g])
                if g == 0:
                    p2.wait("tensor", "DV", dv_h1)
                if bank_free[b] is not None:
                    p2.wait("tensor", "DV", bank_free[b])
                for c in range(KC):
                    p2.op("tensor", (lambda g_, b_, c_, bi_: lambda e, r: e.matmul(
                        pb[b_][:], wbuf[bi_][:, c_ * 128:(c_ + 1) * 128],
                        chunk(h1T, c_), start=(c_ == 0), stop=(c_ == KC - 1)))(
                        g, b, c, bi), "PE", 1 if c == KC - 1 else 0)
                wb_free[bi] = p2.val("PE")
                p2.wait("vector", "PE", p2.val("PE"))
                p2.op("vector", (lambda g_, b_: lambda e, r: e.tensor_scalar(
                    out=chunk(aT, g_), in0=pb[b_][:],
                    scalar1=bias_sb[:, g_:g_ + 1], scalar2=None, op0=ALU.add))(g, b),
                    "DV", 1)
                bank_free[b] = p2.val("DV")
            dv_aT = p2.val("DV")
            pe_done = []
            p2.wait("vector", "DV", dv_aT)
            p2.wait("scalar", "DV", dv_aT)
            emit_ln_stats(aT, False, "ln2")
            for c in range(KC):
                p2.op("vector", (lambda cc: lambda e, r: e.tensor_tensor(
                    out=tmp, in0=chunk(aT, cc), in1=mb_sb, op=ALU.subtract))(c), "DV", 1)
                p2.op("vector", (lambda cc: lambda e, r: e.tensor_tensor(
                    out=tmp, in0=tmp, in1=rb_sb, op=ALU.mult))(c), "DV", 1)
                p2.op("vector", (lambda cc: lambda e, r: e.tensor_tensor(
                    out=chunk(xT, cc), in0=chunk(xT, cc), in1=tmp, op=ALU.add))(c),
                    "DV", 1)
            dv_x2 = p2.val("DV")
            pe_done = []
            p2.wait("scalar", "DV", dv_x2)
            emit_ln_stats(xT, False, "ln3")
            for c in range(KC):
                p2.op("vector", (lambda cc: lambda e, r: e.tensor_tensor(
                    out=tmp, in0=chunk(xT, cc), in1=mb_sb, op=ALU.subtract))(c), "DV", 1)
                p2.op("vector", (lambda cc: lambda e, r: e.tensor_tensor(
                    out=chunk(h1T, cc), in0=tmp, in1=rb_sb, op=ALU.mult))(c), "DV", 1)
            dv_h2 = p2.val("DV")
            p2.wait("gpsimd", "DV", dv_h2)
            p2.op("gpsimd", lambda e, r: e.dma_start(out=x2scr.ap(), in_=xT), "QG", 16)
            qg_x2 = p2.val("QG")
            p2.mark(f"sp{sp}_B")

            gbuf = [axb(0, 2048), axb(2048, 2048)]
            ubuf = [axb(4096, 2048), axb(6144, 2048)]
            g_sb = [axf(8192, 512), axf(8704, 512)]
            guT = agb(0, 44032)
            p2.wait("sync", "QG", qg_x2)
            gb_free = [None, None]
            gbank_free = [None, None]
            ubank_free = [None, None]
            for ic in range(IC):
                bi = ic % 2
                if gb_free[bi] is not None:
                    p2.wait("sync", "PE", gb_free[bi])
                p2.op("sync", (lambda i_, b_: lambda e, r: e.dma_start(
                    out=gbuf[b_], in_=wg_img.ap()[i_]))(ic, bi), "QW", 16)
                p2.op("sync", (lambda i_, b_: lambda e, r: e.dma_start(
                    out=ubuf[b_], in_=wu_img.ap()[i_]))(ic, bi), "QW", 16)
                qw_here = p2.val("QW")
                gb_ = bi
                ub_ = 2 + bi
                p2.wait("tensor", "QW", qw_here)
                if ic == 0:
                    p2.wait("tensor", "DV", dv_h2)
                if gbank_free[bi] is not None:
                    p2.wait("tensor", "AC", gbank_free[bi])
                for c in range(KC):
                    p2.op("tensor", (lambda i_, b_, c_, bb: lambda e, r: e.matmul(
                        pb[bb][:], gbuf[b_][:, c_ * 128:(c_ + 1) * 128],
                        chunk(h1T, c_), start=(c_ == 0), stop=(c_ == KC - 1)))(
                        ic, bi, c, gb_), "PE", 1 if c == KC - 1 else 0)
                pe_gate = p2.val("PE")
                if ubank_free[bi] is not None:
                    p2.wait("tensor", "DV", ubank_free[bi])
                for c in range(KC):
                    p2.op("tensor", (lambda i_, b_, c_, bb: lambda e, r: e.matmul(
                        pb[bb][:], ubuf[b_][:, c_ * 128:(c_ + 1) * 128],
                        chunk(h1T, c_), start=(c_ == 0), stop=(c_ == KC - 1)))(
                        ic, bi, c, ub_), "PE", 1 if c == KC - 1 else 0)
                gb_free[bi] = p2.val("PE")
                p2.wait("scalar", "PE", pe_gate)
                if ic >= 2:
                    p2.wait("scalar", "DV", gu_done[ic - 2])
                p2.op("scalar", (lambda b_, bb: lambda e, r: e.activation(
                    out=g_sb[b_], in_=pb[bb][:], func=AF.Silu))(bi, gb_), "AC", 1)
                gbank_free[bi] = p2.val("AC")
                p2.wait("vector", "AC", p2.val("AC"))
                p2.wait("vector", "PE", gb_free[bi])
                p2.op("vector", (lambda i_, b_, bb: lambda e, r: e.tensor_tensor(
                    out=chunk(guT, i_), in0=pb[bb][:], in1=g_sb[b_], op=ALU.mult))(
                    ic, bi, ub_), "DV", 1)
                ubank_free[bi] = p2.val("DV")
                if ic == 0:
                    gu_done = []
                gu_done.append(p2.val("DV"))
            dv_gu = p2.val("DV")
            pe_gateup = p2.val("PE")
            p2.mark(f"sp{sp}_C")

            dbuf = [axb(0, 5504), axb(5504, 5504)]
            x2buf = [axf(11008, 2048), axf(13056, 2048)]
            ochunk = [axf(15104, 512), axf(15616, 512)]
            fT = h1T
            db_free = [pe_gateup, pe_gateup]
            dbank_free = [None] * 4
            qw_d = []
            for hg in range(32):
                bi = hg % 2
                p2.wait("sync", "PE", db_free[bi])
                if hg < 2:
                    p2.wait("sync", "DV", dv_gu)
                p2.op("sync", (lambda h_, b_: lambda e, r: e.dma_start(
                    out=dbuf[b_], in_=wd_img.ap()[h_]))(hg, bi), "QW", 16)
                qw_d.append(p2.val("QW"))
                b = hg % 4
                p2.wait("tensor", "QW", qw_d[hg])
                if hg == 0:
                    p2.wait("tensor", "DV", dv_gu)
                if dbank_free[b] is not None:
                    p2.wait("tensor", "DV", dbank_free[b])
                for icc in range(IC):
                    p2.op("tensor", (lambda h_, b_, i_, bb: lambda e, r: e.matmul(
                        pb[bb][:], dbuf[b_][:, i_ * 128:(i_ + 1) * 128],
                        chunk(guT, i_), start=(i_ == 0), stop=(i_ == IC - 1)))(
                        hg, bi, icc, b), "PE", 1 if icc == IC - 1 else 0)
                db_free[bi] = p2.val("PE")
                p2.wait("vector", "PE", p2.val("PE"))
                if hg == 0:
                    p2.wait("vector", "DV", dv_gu)
                p2.op("vector", (lambda h_, bb: lambda e, r: e.tensor_copy(
                    out=chunk(fT, h_), in_=pb[bb][:]))(hg, b), "DV", 1)
                dbank_free[b] = p2.val("DV")
            pe_down = p2.val("PE")
            dv_fT = p2.val("DV")
            pe_done = []
            p2.wait("vector", "DV", dv_fT)
            p2.wait("scalar", "DV", dv_fT)
            emit_ln_stats(fT, True, "ln4")
            out_stores_base = p2.val("QG")
            for blk in range(8):
                p2.wait("gpsimd", "QG", qg_x2)
                if blk >= 2:
                    p2.wait("gpsimd", "DV", final_done[blk - 2])
                p2.op("gpsimd", (lambda bl, b_: lambda e, r: e.dma_start(
                    out=x2buf[b_], in_=x2scr.ap()[:, bl * 2048:(bl + 1) * 2048]))(
                    blk, blk % 2), "QG", 16)
                qg_blk = p2.val("QG")
                for j in range(4):
                    c = blk * 4 + j
                    ob = c % 2
                    p2.wait("vector", "QG", qg_blk)
                    if c >= 2:
                        p2.wait("vector", "QG", ostore_done[c - 2])
                    p2.op("vector", (lambda cc: lambda e, r: e.tensor_copy(
                        out=tmp, in_=chunk(fT, cc)))(c), "DV", 1)
                    p2.op("vector", lambda e, r: e.tensor_tensor(
                        out=tmp, in0=tmp, in1=mb_sb, op=ALU.subtract), "DV", 1)
                    p2.op("vector", lambda e, r: e.tensor_tensor(
                        out=tmp, in0=tmp, in1=rb_sb, op=ALU.mult), "DV", 1)
                    p2.op("vector", (lambda bl, j_, o_: lambda e, r: e.tensor_tensor(
                        out=ochunk[o_], in0=tmp,
                        in1=x2buf[bl % 2][:, j_ * 512:(j_ + 1) * 512], op=ALU.add))(
                        blk, j, ob), "DV", 1)
                    if blk == 0 and j == 0:
                        final_done = []
                        ostore_done = []
                    if j == 3:
                        final_done.append(p2.val("DV"))
                    p2.wait("gpsimd", "DV", p2.val("DV"))
                    p2.op("gpsimd", (lambda s_, cc, o_: lambda e, r: e.dma_start(
                        out=out_img.ap()[s_, cc], in_=ochunk[o_]))(sp, c, ob),
                        "QG", 16)
                    ostore_done.append(p2.val("QG"))
            last_sp = dict(out_stores=p2.val("QG"), dv_final=p2.val("DV"),
                           pe_down=pe_down)
            p2.mark(f"sp{sp}_D")
        p2.wait("gpsimd", "QG", p2.val("QG"))

        main_streams = p2.streams

        prep_cut = main_cut = None
        skip_main = False
        if cut is not None:
            if cut in p.marks:
                prep_cut, cut_cnt = p.marks[cut]
                skip_main = True
                dump_prog = p
            else:
                main_cut, cut_cnt = p2.marks[cut]
                dump_prog = p2
            if dump is not None:
                tgt = dump_prog.streams["gpsimd"]
                extra = []
                for snm in sem_names:
                    extra.append(("w", snm, cut_cnt.get(snm, 0)))
                def d_ax(e, r):
                    return e.dma_start(
                        out=out_img.ap()[0].rearrange("c p t -> p c t"),
                        in_=axf(0, 16384).rearrange("p (c t) -> p c t", t=512))
                extra.append(("o", d_ax, "QG", 16))
                if dump == "IMGA":
                    def d3(e, r):
                        return e.dma_start(out=out_img.ap().flatten(),
                                           in_=attn_img.ap()[0:8].flatten())
                    extra.append(("o", d3, "QG", 16))
                elif dump == "AH":
                    def d2(e, r):
                        return e.dma_start(
                            out=out_img.ap()[1].rearrange("c p t -> p c t")[:, 0:16, :],
                            in_=AH.ap().bitcast(F32).rearrange(
                                "p (c t) -> p c t", t=512))
                    extra.append(("o", d2, "QG", 16))
                else:
                    def d2(e, r):
                        return e.dma_start(
                            out=out_img.ap()[1].rearrange("c p t -> p c t"),
                            in_=agf(0, 32768).rearrange("p (c t) -> p c t", t=512))
                    extra.append(("o", d2, "QG", 16))
                extra.append(("w", "QG", cut_cnt.get("QG", 0) + 32))
                if skip_main:
                    sl = prep_cut["gpsimd"]
                    p.streams["gpsimd"] = tgt[:sl] + extra
                    prep_cut = dict(prep_cut, gpsimd=sl + len(extra))
                else:
                    sl = main_cut["gpsimd"]
                    p2.streams["gpsimd"] = tgt[:sl] + extra
                    main_cut = dict(main_cut, gpsimd=sl + len(extra))
        regs = {}
        with nc.Block() as b1:
            for en in ENGS:
                if not prep_streams[en]:
                    continue
                def mk(en_):
                    def body(eng):
                        pr = Prog(); pr.streams = prep_streams
                        pr.replay(en_, eng, sems, regs, prep_cut)
                    return body
                getattr(b1, en)(mk(en))

        nc.all_core_barrier()

        if not skip_main:
            with nc.Block() as b2:
                for en in ENGS:
                    if not main_streams[en]:
                        continue
                    def mk2(en_):
                        def body(eng):
                            pr = Prog(); pr.streams = main_streams
                            pr.replay(en_, eng, sems, regs, main_cut)
                        return body
                    getattr(b2, en)(mk2(en))

    return nc


def _consts():
    ident = np.eye(128, dtype=np.float32).astype(bfdt)
    onesc = np.ones((128, 1), dtype=np.float32).astype(bfdt)
    onesr = np.ones((1, 128), dtype=np.float32)
    selbig = np.zeros((128, 16 * 128), dtype=np.float32)
    for r in range(16):
        for pp in range(128):
            m = 4 * r + pp // 32
            selbig[pp, r * 128 + m] = 1.0
    selexp = np.zeros((128, 16 * 128), dtype=np.float32)
    for gl in range(16):
        for n in range(128):
            selexp[4 * gl + n // 32, gl * 128 + n] = 1.0
    return ident, onesc, onesr, selbig.astype(bfdt), selexp.astype(bfdt)


def kernel(L0_attn_data, L0_attn_mask, L0_attn_weight, L0_attn_bias,
           L0_ffn_w_gate, L0_ffn_w_up, L0_ffn_w_down):
    x = np.ascontiguousarray(np.asarray(L0_attn_data, dtype=np.float32)
                             .reshape(8192, D))
    mask = np.ascontiguousarray(np.asarray(L0_attn_mask, dtype=np.int32))
    w = np.ascontiguousarray(np.asarray(L0_attn_weight, dtype=np.float32))
    bias = np.asarray(L0_attn_bias, dtype=np.float32)
    wg = np.ascontiguousarray(np.asarray(L0_ffn_w_gate, dtype=np.float32))
    wu = np.ascontiguousarray(np.asarray(L0_ffn_w_up, dtype=np.float32))
    wd = np.ascontiguousarray(np.asarray(L0_ffn_w_down, dtype=np.float32))

    ident, onesc, onesr, selbig, selexp = _consts()
    bias_img = np.ascontiguousarray(bias.reshape(KC, 128).T)

    nc = build_program()
    in_maps = []
    for c in range(8):
        rank = c % 2
        xc = x[c * TOK:(c + 1) * TOK]
        xt = np.ascontiguousarray(
            xc.reshape(NSP, T, KC, 128).transpose(0, 3, 2, 1))
        in_maps.append(dict(
            xt=xt,
            w_sl=np.ascontiguousarray(w[rank * 2048:(rank + 1) * 2048]),
            mask_sl=np.ascontiguousarray(mask[rank * 2048:(rank + 1) * 2048]),
            wg_sl=np.ascontiguousarray(wg[rank * 5504:(rank + 1) * 5504]),
            wu_sl=np.ascontiguousarray(wu[rank * 5504:(rank + 1) * 5504]),
            wd_sl=np.ascontiguousarray(wd[rank * 2048:(rank + 1) * 2048]),
            bias_img=bias_img, selbig=selbig, selexp=selexp,
            ident_bf=ident, ones_col=onesc, ones_row=onesr,
        ))
    res = run_bass_kernel_spmd(nc, in_maps, core_ids=list(range(8)))
    outs = []
    for c in range(8):
        oi = res.results[c]["out_img"]
        outs.append(np.asarray(oi, dtype=np.float32)
                    .transpose(0, 3, 1, 2).reshape(TOK, D))
    return np.concatenate(outs, axis=0).reshape(4, 2048, D)
